# version 01
# baseline (speedup 1.0000x reference)
import numpy as np
import ml_dtypes
import concourse.bass as bass
import concourse.mybir as mybir
from concourse.bass_utils import run_bass_kernel_spmd
from concourse.bass import IndirectOffsetOnAxis

F32 = mybir.dt.float32
BF16 = mybir.dt.bfloat16
U32 = mybir.dt.uint32
AF = mybir.ActivationFunctionType
ALU = mybir.AluOpType
AX = mybir.AxisListType

BF = ml_dtypes.bfloat16
NEG = -3.0e38

B, T, D = 4, 2048, 2048
R = 1024
NT = 8
DCH = 16
G_EVEN = [0, 3, 4, 7, 8, 11, 12, 15]
G_ODD = [1, 2, 5, 6, 9, 10, 13, 14]
CHUNKS = [1, 1, 2, 2, 3, 3, 4, 4]

IDENT128 = np.eye(128).astype(BF)


def keys_pad(kp, half):
    out = np.zeros((128, 512), kp.dtype)
    if half == 0:
        out[0:64] = kp[0:64]
    else:
        out[64:128] = kp[64:128]
    return out


def pack_kmajor(w):
    k, n = w.shape
    return np.ascontiguousarray(
        w.reshape(k // 128, 128, n).transpose(1, 0, 2).reshape(128, -1))

_launch_count = 0


def core_rows(c):
    tiles = G_EVEN if c % 2 == 0 else G_ODD
    return np.concatenate([np.arange(g * 128, (g + 1) * 128) for g in tiles])


def run(nc, in_maps):
    global _launch_count
    _launch_count += 1
    return run_bass_kernel_spmd(nc, in_maps, core_ids=list(range(8)))


ENGS = ("sync", "tensor", "vector", "scalar", "gpsimd")
ESEM = {"sync": "S", "tensor": "T", "vector": "V", "scalar": "A",
        "gpsimd": "G"}
DMA_ENG = {"sync", "gpsimd"}


class Prog:
    def __init__(self):
        self.nc = bass.Bass("TRN2", target_bir_lowering=False)
        self.ops = {e: [] for e in ENGS}
        self.cnt = {}
        self.sems = {}
        self.sem_names = []
        self.last_write = {}
        self.readers = {}
        self.hw = {e: {} for e in ENGS}
        self.dma_sems = set()

    def _sem(self, name):
        if name not in self.cnt:
            self.cnt[name] = 0
            self.sem_names.append(name)
        return name

    def _emit_wait(self, eng, sem, val):
        if val <= 0:
            return
        if self.hw[eng].get(sem, 0) >= val:
            return
        self.hw[eng][sem] = val

        def f(e, sem=sem, val=val):
            e.wait_ge(self.sems[sem], val)
        self.ops[eng].append(f)

    def op(self, eng, emit_fn, reads=(), writes=(), writes_pre=(),
           writes_post=(), n=None, sem=None):
        if eng in DMA_ENG:
            assert sem is not None, "DMA ops need a dedicated sem"
        sem = self._sem(sem or ESEM[eng])
        if n is None:
            n = 16 if eng in DMA_ENG else 1
        if eng in DMA_ENG:
            self.dma_sems.add(sem)
        writes_pre = list(writes_pre) + list(writes)
        writes_post = list(writes_post) + list(writes)
        for r in reads:
            lw = self.last_write.get(r)
            if lw is not None:
                self._emit_wait(eng, lw[0], lw[1])
        for w in writes_pre:
            lw = self.last_write.get(w)
            if lw is not None:
                self._emit_wait(eng, lw[0], lw[1])
            for (s2, c2) in self.readers.get(w, ()):
                self._emit_wait(eng, s2, c2)
            self.readers[w] = []
        self.cnt[sem] += n
        post = self.cnt[sem]

        def f(e, emit_fn=emit_fn, sem=sem, n=n):
            emit_fn(e).then_inc(self.sems[sem], n)
        self.ops[eng].append(f)
        for w in writes_post:
            self.last_write[w] = (sem, post)
        for r in reads:
            self.readers.setdefault(r, []).append((sem, post))
        return post

    def fence_all_dma(self, eng):
        for s in self.sem_names:
            if s in self.dma_sems:
                self._emit_wait(eng, s, self.cnt[s])

    def fence_reads(self, eng, regions):
        for r in regions:
            lw = self.last_write.get(r)
            if lw is not None:
                self._emit_wait(eng, lw[0], lw[1])

    def finalize(self):
        nc = self.nc
        import contextlib
        stack = contextlib.ExitStack()
        for s in self.sem_names:
            nm = s if isinstance(s, str) else "_".join(str(q) for q in s)
            self.sems[s] = stack.enter_context(nc.semaphore(nm))
        with stack:
            with nc.Block() as blk:
                for eng in ENGS:
                    fns = self.ops[eng]
                    if not fns:
                        continue
                    getattr(blk, eng)(lambda e, fns=fns: [f(e) for f in fns])
        return nc


def build_convert(specs):
    p = Prog()
    nc = p.nc
    FW = 2048
    tiles = []
    for name, nr, ncol in specs:
        fin = nc.dram_tensor(name, [nr, ncol], F32, kind="ExternalInput").ap()
        fout = nc.dram_tensor(name + "_b", [nr, ncol], BF16,
                              kind="ExternalOutput").ap()
        n = nr * ncol
        assert n % 128 == 0
        per = n // 128
        w = FW
        while per % w != 0:
            w //= 2
        nch = per // w
        vin = fin.rearrange("r c -> (r c)").rearrange(
            "(n p w) -> n p w", p=128, w=w)
        vout = fout.rearrange("r c -> (r c)").rearrange(
            "(n p w) -> n p w", p=128, w=w)
        for i in range(nch):
            tiles.append((vin[i], vout[i], w))

    nbuf = 4
    bufs = [nc.alloc_sbuf_tensor(f"cv{i}", [128, FW], F32) for i in range(nbuf)]
    obufs = [nc.alloc_sbuf_tensor(f"cvo{i}", [128, FW], BF16)
             for i in range(nbuf)]
    for i, (vin, vout, w) in enumerate(tiles):
        b = i % nbuf
        p.op("sync", lambda e, vin=vin, b=b, w=w:
             e.dma_start(bufs[b][:, :w], vin), writes=[("cv", b)],
             sem=("ld", b))
        p.op("vector", lambda e, b=b, w=w:
             e.tensor_copy(obufs[b][:, :w], bufs[b][:, :w]),
             reads=[("cv", b)], writes=[("cvo", b)])
        p.op("gpsimd", lambda e, vout=vout, b=b, w=w:
             e.dma_start(vout, obufs[b][:, :w]), reads=[("cvo", b)],
             writes=[("out", i)], sem=("st", b))
    p.fence_all_dma("gpsimd")
    return p.finalize()


def convert_weights(inputs):
    host = {}

    def prep(name, arr):
        host[name] = np.ascontiguousarray(arr, dtype=np.float32)

    prep("l0wq", inputs["L0_attn_Wq"])
    prep("l0wk", inputs["L0_attn_Wk"])
    prep("l0wv", inputs["L0_attn_Wv"])
    prep("l0wo", inputs["L0_attn_Wproj"])
    prep("fwq", inputs["L0_ffn_Wq"])
    prep("edown", inputs["L0_ffn_e_down"])
    prep("eup", inputs["L0_ffn_e_up"])
    keys = inputs["L0_ffn_keys"]
    kp = np.zeros((128, 4 * 128), np.float32)
    for h in range(4):
        kp[0:64, h * 128:(h + 1) * 128] = keys[h, :, 0, :].T
        kp[64:128, h * 128:(h + 1) * 128] = keys[h, :, 1, :].T
    prep("keysp", kp)
    perm = np.zeros(D, np.int64)
    for h in range(16):
        base = h * 128
        perm[base:base + 64] = base + 2 * np.arange(64)
        perm[base + 64:base + 128] = base + 2 * np.arange(64) + 1
    prep("l1wq", inputs["L1_attn_Wq"][perm].T)
    prep("l1wk", inputs["L1_attn_Wk"][perm].T)
    prep("l1wv", inputs["L1_attn_Wv"].T)
    prep("l1wo", inputs["L1_attn_Wo"].T)
    prep("gateT", inputs["L1_ffn_gate_w"].T.copy())
    prep("w1T", inputs["L1_ffn_w1"].transpose(0, 2, 1).reshape(8 * D, 1024))
    prep("w2T", inputs["L1_ffn_w2"].transpose(0, 2, 1).reshape(8 * 1024, D))
    prep("w3T", inputs["L1_ffn_w3"].transpose(0, 2, 1).reshape(8 * D, 1024))
    prep("sw1T", inputs["L1_ffn_sw1"].T)
    prep("sw2T", inputs["L1_ffn_sw2"].T)
    prep("sw3T", inputs["L1_ffn_sw3"].T)

    specs = [(k, v.shape[0], v.shape[1]) for k, v in host.items()]
    nc = build_convert(specs)
    in_maps = [dict(host) for _ in range(8)]
    res = run(nc, in_maps)
    return {k: res.results[0][k + "_b"] for k in host}


def emit_ln(p, xt, xreg, sb_xn, sb_sq, sb_st):
    p.op("vector", lambda e: e.reduce_sum(sb_st[:, 0:1], xt, AX.X),
         reads=[xreg], writes=["st0"])
    p.op("scalar", lambda e: e.activation(
        sb_sq[:], xt, AF.Square, bias=0.0, scale=1.0,
        accum_out=sb_st[:, 1:2]), reads=[xreg], writes=["sq", "st1"])
    p.op("vector", lambda e: e.tensor_tensor(
        sb_st[:, 2:3], sb_st[:, 0:1], sb_st[:, 0:1], ALU.mult),
        reads=["st0"], writes=["st2"])
    p.op("vector", lambda e: e.scalar_tensor_tensor(
        sb_st[:, 3:4], sb_st[:, 2:3], 1.0 / 2048.0, sb_st[:, 1:2],
        ALU.mult, ALU.subtract), reads=["st1", "st2"], writes=["st3"])
    p.op("vector", lambda e: e.tensor_scalar(
        sb_st[:, 4:5], sb_st[:, 3:4], -1.0 / 2048.0, 1e-5, ALU.mult, ALU.add),
        reads=["st3"], writes=["st4"])
    p.op("scalar", lambda e: e.activation(
        sb_st[:, 5:6], sb_st[:, 4:5], AF.Sqrt, bias=0.0, scale=1.0),
        reads=["st4"], writes=["st5"])
    p.op("vector", lambda e: e.reciprocal(sb_st[:, 6:7], sb_st[:, 5:6]),
         reads=["st5"], writes=["st6"])
    p.op("vector", lambda e: e.tensor_scalar(
        sb_st[:, 7:8], sb_st[:, 0:1], 1.0 / 2048.0, None, ALU.mult),
        reads=["st0"], writes=["st7"])
    p.op("vector", lambda e: e.tensor_scalar(
        sb_xn[:], xt, sb_st[:, 7:8], sb_st[:, 6:7],
        ALU.subtract, ALU.mult), reads=[xreg, "st6", "st7"], writes=["xn"])


def emit_transpose(p, src, src_reg, dst, dst_reg, ps_tr, sb_ident, n=16):
    for i in range(n):
        p.op("tensor", lambda e, i=i: e.transpose(
            ps_tr[:], src[:, i * 128:(i + 1) * 128], sb_ident[:]),
            reads=[src_reg, "ident"], writes=["ps_tr"])
        p.op("vector", lambda e, i=i: e.tensor_copy(
            dst[:, i * 128:(i + 1) * 128], ps_tr[:]),
            reads=["ps_tr"], writes=[(dst_reg, i)])


def build_l0ffn(stage=3, debug=False):
    p = Prog()
    nc = p.nc
    x_in = nc.dram_tensor("x", [R, D], F32, kind="ExternalInput").ap()
    wq = nc.dram_tensor("fwq", [128, DCH * 512], BF16,
                        kind="ExternalInput").ap()
    keysA = nc.dram_tensor("keysA", [128, 512], BF16, kind="ExternalInput").ap()
    keysB = nc.dram_tensor("keysB", [128, 512], BF16, kind="ExternalInput").ap()
    edown = nc.dram_tensor("edown", [16384, D], BF16, kind="ExternalInput").ap()
    eup = nc.dram_tensor("eup", [16384, D], BF16, kind="ExternalInput").ap()
    ident_in = nc.dram_tensor("ident", [128, 128], BF16,
                              kind="ExternalInput").ap()
    x_out = nc.dram_tensor("xo", [R, D], F32, kind="ExternalOutput").ap()
    if debug:
        dbg_q = nc.dram_tensor("dbg_q", [R, 512], BF16,
                               kind="ExternalOutput").ap()
        dbg_sims = nc.dram_tensor("dbg_sims", [R, 4, 256], F32,
                                  kind="ExternalOutput").ap()
        dbg_idx = nc.dram_tensor("dbg_idx", [R, 32], U32,
                                 kind="ExternalOutput").ap()
        dbg_fs = nc.dram_tensor("dbg_fs", [R, 32], F32,
                                kind="ExternalOutput").ap()
        dbg_g = nc.dram_tensor("dbg_g", [R, 32], F32,
                               kind="ExternalOutput").ap()
        dbg_hid = nc.dram_tensor("dbg_hid", [R, 32], F32,
                                 kind="ExternalOutput").ap()

    sb_x = nc.alloc_sbuf_tensor("sb_x", [128, NT * D], F32)
    sb_xn = nc.alloc_sbuf_tensor("sb_xn", [128, D], BF16)
    sb_sq = nc.alloc_sbuf_tensor("sb_sq", [128, D], F32)
    sb_st = nc.alloc_sbuf_tensor("sb_st", [128, 8], F32)
    sb_xnT = nc.alloc_sbuf_tensor("sb_xnT", [128, D], BF16)
    sb_ident = nc.alloc_sbuf_tensor("sb_ident", [128, 128], BF16)
    sb_wq = nc.alloc_sbuf_tensor("sb_wq", [128, DCH * 512], BF16)
    sb_keys = nc.alloc_sbuf_tensor("sb_keys", [128, 1024], BF16)
    sb_qT = nc.alloc_sbuf_tensor("sb_qT", [128, 512], BF16)
    sb_sims = nc.alloc_sbuf_tensor("sb_sims", [128, 256], F32)
    sb_s8 = nc.alloc_sbuf_tensor("sb_s8", [128, 16], F32)
    sb_i8 = nc.alloc_sbuf_tensor("sb_i8", [128, 16], U32)
    sb_i8f = nc.alloc_sbuf_tensor("sb_i8f", [128, 16], F32)
    sb_osc = nc.alloc_sbuf_tensor("sb_osc", [128, 64], F32)
    sb_oix = nc.alloc_sbuf_tensor("sb_oix", [128, 64], F32)
    sb_fs = nc.alloc_sbuf_tensor("sb_fs", [128, 32], F32)
    sb_relu = nc.alloc_sbuf_tensor("sb_relu", [128, 32], F32)
    sb_selt = nc.alloc_sbuf_tensor("sb_selt", [128, 64], F32)
    sb_idxf = nc.alloc_sbuf_tensor("sb_idxf", [128, 32], F32)
    sb_idx = nc.alloc_sbuf_tensor("sb_idx", [128, 32], U32)
    sb_wd = [nc.alloc_sbuf_tensor(f"sb_wd{i}", [128, D], BF16)
             for i in range(2)]
    sb_wu = [nc.alloc_sbuf_tensor(f"sb_wu{i}", [128, D], BF16)
             for i in range(2)]
    sb_hid = nc.alloc_sbuf_tensor("sb_hid", [128, 32], F32)
    sb_ttr = nc.alloc_sbuf_tensor("sb_ttr", [128, D], F32)
    sb_g = nc.alloc_sbuf_tensor("sb_g", [128, 32], F32)
    sb_M = nc.alloc_sbuf_tensor("sb_M", [128, 128], BF16)

    ps_tr = nc.alloc_psum_tensor("ps_tr", [128, 128], BF16)
    ps_q = nc.alloc_psum_tensor("ps_q", [128, 512], F32)
    ps_sim = nc.alloc_psum_tensor("ps_sim", [128, 256], F32)
    ps_out = nc.alloc_psum_tensor("ps_out", [128, 2048], F32)

    regs = {}

    def mkreg(e):
        regs["bc"] = e.to_reg(16383)
    p.ops["gpsimd"].append(mkreg)
    p.op("sync", lambda e: e.dma_start(sb_ident[:], ident_in),
         writes=["ident"], sem="ld_id")
    p.op("sync", lambda e: e.dma_start(sb_wq[:], wq), writes=["wq"],
         sem="ld_wq")
    p.op("sync", lambda e: e.dma_start(sb_keys[:, 0:512], keysA),
         writes=["keysA"], sem="ld_ky")
    p.op("sync", lambda e: e.dma_start(sb_keys[:, 512:1024], keysB),
         writes=["keysB"], sem="ld_ky2")
    for t in range(NT):
        p.op("sync", lambda e, t=t: e.dma_start(
            sb_x[:, t * D:(t + 1) * D], x_in[t * 128:(t + 1) * 128, :]),
            writes=[("x", t)], sem=("ld_x", t))

    for t in range(NT):
        xt = sb_x[:, t * D:(t + 1) * D]
        emit_ln(p, xt, ("x", t), sb_xn, sb_sq, sb_st)
        emit_transpose(p, sb_xn, "xn", sb_xnT, "xnT", ps_tr, sb_ident)
        for oc in range(4):
            for dc in range(DCH):
                p.op("tensor", lambda e, oc=oc, dc=dc: e.matmul(
                    ps_q[:, oc * 128:(oc + 1) * 128],
                    sb_wq[:, dc * 512 + oc * 128: dc * 512 + (oc + 1) * 128],
                    sb_xnT[:, dc * 128:(dc + 1) * 128],
                    start=(dc == 0), stop=(dc == DCH - 1)),
                    reads=["wq", ("xnT", dc)],
                    writes_pre=(["ps_q"] if (oc == 0 and dc == 0) else ()),
                    writes_post=(["ps_q"] if (oc == 3 and dc == DCH - 1)
                                 else ()))
        p.op("vector", lambda e: e.tensor_copy(sb_qT[:], ps_q[:]),
             reads=["ps_q"], writes=["qT"])
        if debug:
            p.op("sync", lambda e, t=t: e.dma_start(
                dbg_q[t * 128:(t + 1) * 128, :], sb_qT[:]), reads=["qT"],
                sem="db_q")
        if stage < 1:
            continue
        for h in range(4):
            p.op("tensor", lambda e, h=h: e.matmul(
                ps_sim[:, 0:128], sb_qT[:, h * 128:(h + 1) * 128],
                sb_keys[:, h * 128:(h + 1) * 128], start=True, stop=True),
                reads=["qT", "keysA"], writes_pre=["ps_sim"])
            p.op("tensor", lambda e, h=h: e.matmul(
                ps_sim[:, 128:256], sb_qT[:, h * 128:(h + 1) * 128],
                sb_keys[:, 512 + h * 128:512 + (h + 1) * 128],
                start=True, stop=True),
                reads=["qT", "keysB"], writes_post=["ps_sim"])
            p.op("vector", lambda e: e.tensor_copy(sb_sims[:], ps_sim[:]),
                 reads=["ps_sim"], writes=["sims"])
            if debug:
                p.op("sync", lambda e, t=t, h=h: e.dma_start(
                    dbg_sims[t * 128:(t + 1) * 128, h, :], sb_sims[:]),
                    reads=["sims"], sem="db_sims")
            p.op("vector", lambda e: e.max(
                out=sb_s8[:, 0:8], in_=sb_sims[:, 0:128]),
                reads=["sims"], writes=["s8a"])
            p.op("vector", lambda e: e.max(
                out=sb_s8[:, 8:16], in_=sb_sims[:, 128:256]),
                reads=["sims"], writes=["s8b"])
            p.op("vector", lambda e: e.max_index(
                sb_i8[:, 0:8], sb_s8[:, 0:8], sb_sims[:, 0:128]),
                reads=["sims", "s8a"], writes=["i8a"])
            p.op("vector", lambda e: e.max_index(
                sb_i8[:, 8:16], sb_s8[:, 8:16], sb_sims[:, 128:256]),
                reads=["sims", "s8b"], writes=["i8b"])
            p.op("vector", lambda e: e.tensor_copy(sb_i8f[:], sb_i8[:]),
                 reads=["i8a", "i8b"], writes=["i8f"])
            p.op("vector", lambda e: e.tensor_tensor(
                sb_osc[:].rearrange("p (a b) -> p a b", a=8),
                sb_s8[:, 0:8].unsqueeze(2).to_broadcast([128, 8, 8]),
                sb_s8[:, 8:16].unsqueeze(1).to_broadcast([128, 8, 8]),
                ALU.add), reads=["s8a", "s8b"], writes=["osc"])
            p.op("vector", lambda e: e.scalar_tensor_tensor(
                sb_oix[:].rearrange("p (a b) -> p a b", a=8),
                sb_i8f[:, 0:8].unsqueeze(2).to_broadcast([128, 8, 8]),
                128.0,
                sb_i8f[:, 8:16].unsqueeze(1).to_broadcast([128, 8, 8]),
                ALU.mult, ALU.add), reads=["i8f"], writes=["oix"])
            p.op("vector", lambda e, h=h: e.max(
                out=sb_fs[:, h * 8:(h + 1) * 8], in_=sb_osc[:]),
                reads=["osc"], writes=[("fs", h)])
            for k in range(8):
                p.op("vector", lambda e, h=h, k=k: e.scalar_tensor_tensor(
                    sb_selt[:], sb_osc[:], sb_fs[:, h * 8 + k:h * 8 + k + 1],
                    sb_oix[:], ALU.is_equal, ALU.mult,
                    accum_out=sb_idxf[:, h * 8 + k:h * 8 + k + 1]),
                    reads=["osc", "oix", ("fs", h)],
                    writes=["selt", ("idxf", h)])
        p.op("vector", lambda e: e.tensor_copy(sb_idx[:], sb_idxf[:]),
             reads=[("idxf", h) for h in range(4)], writes=["idx"])
        if debug:
            p.op("sync", lambda e, t=t: e.dma_start(
                dbg_idx[t * 128:(t + 1) * 128, :], sb_idx[:]),
                reads=["idx"], sem="db_idx")
            p.op("sync", lambda e, t=t: e.dma_start(
                dbg_fs[t * 128:(t + 1) * 128, :], sb_fs[:]),
                reads=[("fs", h) for h in range(4)], sem="db_fs")
        if stage < 2:
            continue
        for s in range(32):
            bb = s % 2
            p.op("gpsimd", lambda e, s=s, bb=bb: e.indirect_dma_start(
                sb_wd[bb][:], None, edown,
                IndirectOffsetOnAxis(sb_idx[:, s:s + 1], 0),
                bounds_check=regs["bc"], oob_is_err=False),
                reads=["idx"], writes=[("wd", bb)], sem=("g_wd", bb))
            p.op("vector", lambda e, s=s, bb=bb: e.scalar_tensor_tensor(
                sb_ttr[:], sb_wd[bb][:], 1.0, sb_xn[:],
                ALU.mult, ALU.mult,
                accum_out=sb_hid[:, s:s + 1]),
                reads=[("wd", bb), "xn"], writes=["ttr", ("hid", s)])
        hid_regs = [("hid", s) for s in range(32)]
        p.op("scalar", lambda e: e.activation(
            sb_g[:], sb_hid[:], AF.Silu, bias=0.0, scale=1.0),
            reads=hid_regs, writes=["g"])
        if debug:
            p.op("sync", lambda e, t=t: e.dma_start(
                dbg_hid[t * 128:(t + 1) * 128, :], sb_hid[:]),
                reads=hid_regs, sem="db_hid")
        p.op("vector", lambda e: e.tensor_scalar_max(
            sb_relu[:], sb_fs[:], 0.0),
            reads=[("fs", h) for h in range(4)], writes=["relu"])
        p.op("vector", lambda e: e.tensor_tensor(
            sb_g[:], sb_g[:], sb_relu[:], ALU.mult),
            reads=["g", "relu"], writes=["g"])
        if debug:
            p.op("sync", lambda e, t=t: e.dma_start(
                dbg_g[t * 128:(t + 1) * 128, :], sb_g[:]), reads=["g"],
                sem="db_g")
        if stage < 3:
            continue
        for s in range(32):
            bb = s % 2
            p.op("gpsimd", lambda e, s=s, bb=bb: e.indirect_dma_start(
                sb_wu[bb][:], None, eup,
                IndirectOffsetOnAxis(sb_idx[:, s:s + 1], 0),
                bounds_check=regs["bc"], oob_is_err=False),
                reads=["idx"], writes=[("wu", bb)], sem=("g_wu", bb))
            p.op("vector", lambda e, s=s: e.tensor_scalar_mul(
                sb_M[:], sb_ident[:], sb_g[:, s:s + 1]),
                reads=["ident", "g"], writes=["M"])
            for oc in range(4):
                p.op("tensor", lambda e, s=s, bb=bb, oc=oc: e.matmul(
                    ps_out[:, oc * 512:(oc + 1) * 512], sb_M[:],
                    sb_wu[bb][:, oc * 512:(oc + 1) * 512],
                    start=(s == 0), stop=(s == 31)),
                    reads=["M", ("wu", bb)],
                    writes_pre=(["ps_out"] if (s == 0 and oc == 0) else ()),
                    writes_post=(["ps_out"] if (s == 31 and oc == 3) else ()))
        p.op("vector", lambda e, xt=xt: e.scalar_tensor_tensor(
            xt, ps_out[:], 0.5, xt, ALU.mult, ALU.add),
            reads=["ps_out"], writes=[("x", t), "ps_out"])
        p.op("sync", lambda e, t=t: e.dma_start(
            x_out[t * 128:(t + 1) * 128, :], sb_x[:, t * D:(t + 1) * D]),
            reads=[("x", t)], writes=[("xo", t)], sem="st_xo")
    p.fence_all_dma("sync")
    return p.finalize()


def l0ffn_launch(x_by_core, wb, stage=3, debug=False):
    nc = build_l0ffn(stage, debug)
    in_maps = []
    for c in range(8):
        in_maps.append({
            "x": x_by_core[c],
            "fwq": pack_kmajor(wb["fwq"]),
            "keysA": keys_pad(wb["keysp"], 0), "keysB": keys_pad(wb["keysp"], 1),
            "edown": wb["edown"], "eup": wb["eup"],
            "ident": IDENT128,
        })
    res = run(nc, in_maps)
    if debug:
        return res.results
    return [res.results[c]["xo"] for c in range(8)]


def kernel(**inputs):
    global _launch_count
    _launch_count = 0
    wb = convert_weights(inputs)
    l1w = l1ffn_weights(wb)
    x = np.ascontiguousarray(inputs["L0_attn_x"], dtype=np.float32)
    rows = [core_rows(c) for c in range(8)]
    x_by_core = [np.ascontiguousarray(x[c // 2].reshape(T, D)[rows[c]])
                 for c in range(8)]

    x_by_core = l0ffn_launch(x_by_core, wb)
    x_by_core = attn_launch(x_by_core, wb, 0)
    x_by_core = l0ffn_launch(x_by_core, wb)
    x_by_core = l1ffn_launch(x_by_core, l1w)
    x_by_core = attn_launch(x_by_core, wb, 1)
    x_by_core = l1ffn_launch(x_by_core, l1w)

    out = np.zeros((B, T, D), np.float32)
    for c in range(8):
        out[c // 2][rows[c]] = np.asarray(x_by_core[c], np.float32)
    return out


def emit_rope(p, ps, ps_reg, out, out_reg, sb_cos, sb_sin, st, tmp):
    cs = sb_cos[:, st * 512:(st + 1) * 512]
    sn = sb_sin[:, st * 512:(st + 1) * 512]
    t1 = tmp[:, 0:512]
    t2 = tmp[:, 512:1024]
    p.op("vector", lambda e: e.tensor_tensor(t1, ps[0:64, :], cs, ALU.mult),
         reads=[ps_reg, "cosq"], writes=["rt1"])
    p.op("vector", lambda e: e.tensor_tensor(t2, ps[64:128, :], sn, ALU.mult),
         reads=[ps_reg, "cosq"], writes=["rt2"])
    p.op("vector", lambda e: e.tensor_tensor(out[0:64, :], t1, t2,
                                             ALU.subtract),
         reads=["rt1", "rt2"], writes=[out_reg])
    p.op("vector", lambda e: e.tensor_tensor(t1, ps[64:128, :], cs, ALU.mult),
         reads=[ps_reg, "cosq"], writes=["rt1"])
    p.op("vector", lambda e: e.tensor_tensor(t2, ps[0:64, :], sn, ALU.mult),
         reads=[ps_reg, "cosq"], writes=["rt2"])
    p.op("vector", lambda e: e.tensor_tensor(out[64:128, :], t1, t2, ALU.add),
         reads=["rt1", "rt2"], writes=[out_reg])


def build_attn_a(kvd):
    nkv = kvd // 128
    p = Prog()
    nc = p.nc
    x_in = nc.dram_tensor("x", [R, D], F32, kind="ExternalInput").ap()
    wq_in = nc.dram_tensor("wq", [128, 16 * D], BF16, kind="ExternalInput").ap()
    wk_in = nc.dram_tensor("wk", [128, nkv * D], BF16, kind="ExternalInput").ap()
    wv_in = nc.dram_tensor("wv", [128, (kvd // 512) * DCH * 512], BF16,
                           kind="ExternalInput").ap()
    cosq_in = nc.dram_tensor("cosq", [64, R], F32, kind="ExternalInput").ap()
    sinq_in = nc.dram_tensor("sinq", [64, R], F32, kind="ExternalInput").ap()
    cosk_in = nc.dram_tensor("cosk", [64, R], F32, kind="ExternalInput").ap()
    sink_in = nc.dram_tensor("sink", [64, R], F32, kind="ExternalInput").ap()
    ident_in = nc.dram_tensor("ident", [128, 128], BF16,
                              kind="ExternalInput").ap()
    qT_out = nc.dram_tensor("qT", [16, 128, R], BF16, kind="ExternalOutput").ap()
    kT_out = nc.dram_tensor("kT", [nkv, 128, R], BF16,
                            kind="ExternalOutput").ap()
    v_out = nc.dram_tensor("v", [R, kvd], BF16, kind="ExternalOutput").ap()

    sb_x = nc.alloc_sbuf_tensor("sb_x", [128, NT * D], F32)
    sb_xn = nc.alloc_sbuf_tensor("sb_xn", [128, D], BF16)
    sb_sq = nc.alloc_sbuf_tensor("sb_sq", [128, D], F32)
    sb_st = nc.alloc_sbuf_tensor("sb_st", [128, 8], F32)
    sb_xnT = nc.alloc_sbuf_tensor("sb_xnT", [128, DCH * 512], BF16)
    sb_ident = nc.alloc_sbuf_tensor("sb_ident", [128, 128], BF16)
    sb_cosq = nc.alloc_sbuf_tensor("sb_cosq", [64, R], F32)
    sb_sinq = nc.alloc_sbuf_tensor("sb_sinq", [64, R], F32)
    sb_cosk = nc.alloc_sbuf_tensor("sb_cosk", [64, R], F32)
    sb_sink = nc.alloc_sbuf_tensor("sb_sink", [64, R], F32)
    sb_w = [nc.alloc_sbuf_tensor(f"sb_w{i}", [128, DCH * 512], BF16)
            for i in range(2)]
    sb_tmp = nc.alloc_sbuf_tensor("sb_tmp", [64, 1024], F32)
    sb_ev = nc.alloc_sbuf_tensor("sb_ev", [128, 512], BF16)
    ps_tr = nc.alloc_psum_tensor("ps_tr", [128, 128], BF16)
    ps_p = nc.alloc_psum_tensor("ps_p", [128, 512], F32)

    p.op("sync", lambda e: e.dma_start(sb_ident[:], ident_in),
         writes=["ident"], sem="ld_id")
    for nm, sb, src in (("cq", sb_cosq, cosq_in), ("sq", sb_sinq, sinq_in),
                        ("ck", sb_cosk, cosk_in), ("sk", sb_sink, sink_in)):
        p.op("sync", lambda e, sb=sb, src=src: e.dma_start(sb[:], src),
             writes=["cosq"], sem="ld_" + nm)
    for t in range(NT):
        p.op("sync", lambda e, t=t: e.dma_start(
            sb_x[:, t * D:(t + 1) * D], x_in[t * 128:(t + 1) * 128, :]),
            writes=[("x", t)], sem=("ld_x", t))

    for st in range(2):
        for tt in range(4):
            t = st * 4 + tt
            xt = sb_x[:, t * D:(t + 1) * D]
            emit_ln(p, xt, ("x", t), sb_xn, sb_sq, sb_st)
            for i in range(DCH):
                p.op("tensor", lambda e, i=i: e.transpose(
                    ps_tr[:], sb_xn[:, i * 128:(i + 1) * 128], sb_ident[:]),
                    reads=["xn", "ident"], writes=["ps_tr"])
                p.op("vector", lambda e, i=i, tt=tt: e.tensor_copy(
                    sb_xnT[:, i * 512 + tt * 128: i * 512 + (tt + 1) * 128],
                    ps_tr[:]), reads=["ps_tr"], writes=[("xnT", i, tt)])
        xnT_all = [("xnT", i, tt) for i in range(DCH) for tt in range(4)]
        for kind, n_oc, w_in_ap, out_ap, ctabs in (
                ("q", 16, wq_in, qT_out, (sb_cosq, sb_sinq)),
                ("k", nkv, wk_in, kT_out, (sb_cosk, sb_sink))):
            for oc in range(n_oc):
                wb = oc % 2
                p.op("sync", lambda e, oc=oc, wb=wb, w_in_ap=w_in_ap:
                     e.dma_start(sb_w[wb][:, 0:2048],
                                 w_in_ap[:, oc * D:(oc + 1) * D]),
                     writes=[("wbuf", wb)], sem=("ld_w", wb))
                for dc in range(DCH):
                    p.op("tensor", lambda e, oc=oc, dc=dc, wb=wb: e.matmul(
                        ps_p[:], sb_w[wb][:, dc * 128:(dc + 1) * 128],
                        sb_xnT[:, dc * 512:(dc + 1) * 512],
                        start=(dc == 0), stop=(dc == DCH - 1)),
                        reads=[("wbuf", wb)] + [("xnT", dc, q) for q in
                                                range(4)],
                        writes_pre=(["ps_p"] if dc == 0 else ()),
                        writes_post=(["ps_p"] if dc == DCH - 1 else ()))
                emit_rope(p, ps_p, "ps_p", sb_ev, "ev", ctabs[0], ctabs[1],
                          st, sb_tmp)
                p.op("sync", lambda e, oc=oc, st=st, out_ap=out_ap:
                     e.dma_start(out_ap[oc, :, st * 512:(st + 1) * 512],
                                 sb_ev[:]),
                     reads=["ev"], writes=["ps_p"], sem="st_qk")
        for vc in range(kvd // 512):
            wb = vc % 2
            p.op("sync", lambda e, vc=vc, wb=wb: e.dma_start(
                sb_w[wb][:], wv_in[:, vc * DCH * 512:(vc + 1) * DCH * 512]),
                writes=[("wbuf", wb)], sem=("ld_w", wb))
            for tt in range(4):
                t = st * 4 + tt
                for dc in range(DCH):
                    p.op("tensor", lambda e, tt=tt, dc=dc, wb=wb: e.matmul(
                        ps_p[:],
                        sb_xnT[:, dc * 512 + tt * 128: dc * 512 + (tt + 1) * 128],
                        sb_w[wb][:, dc * 512:(dc + 1) * 512],
                        start=(dc == 0), stop=(dc == DCH - 1)),
                        reads=[("wbuf", wb), ("xnT", dc, tt)],
                        writes_pre=(["ps_p"] if dc == 0 else ()),
                        writes_post=(["ps_p"] if dc == DCH - 1 else ()))
                p.op("vector", lambda e: e.tensor_copy(sb_ev[:], ps_p[:]),
                     reads=["ps_p"], writes=["ev", "ps_p"])
                p.op("sync", lambda e, t=t, vc=vc: e.dma_start(
                    v_out[t * 128:(t + 1) * 128, vc * 512:(vc + 1) * 512],
                    sb_ev[:]), reads=["ev"], writes=["evd"], sem="st_v")
    p.fence_all_dma("sync")
    return p.finalize()


def build_attn_b(kvd):
    nkv = kvd // 128
    p = Prog()
    nc = p.nc
    x_in = nc.dram_tensor("x", [R, D], F32, kind="ExternalInput").ap()
    qT_in = nc.dram_tensor("qT", [16, 128, R], BF16, kind="ExternalInput").ap()
    kT_in = nc.dram_tensor("kT", [nkv, 128, T], BF16,
                           kind="ExternalInput").ap()
    v_in = nc.dram_tensor("v", [T, kvd], BF16, kind="ExternalInput").ap()
    wo_in = nc.dram_tensor("wo", [128, 4 * DCH * 512], BF16,
                           kind="ExternalInput").ap()
    mask_in = nc.dram_tensor("mask", [8, 128, 512], BF16,
                             kind="ExternalInput").ap()
    ident_in = nc.dram_tensor("ident", [128, 128], BF16,
                              kind="ExternalInput").ap()
    x_out = nc.dram_tensor("xo", [R, D], F32, kind="ExternalOutput").ap()

    HG = 8
    kvh = kvd // 2 if kvd > 512 else 256
    sb_qT = nc.alloc_sbuf_tensor("sb_qT", [128, 16 * R], BF16)
    sb_kT = nc.alloc_sbuf_tensor("sb_kT", [128, (nkv // 2) * 2048], BF16)
    sb_v = nc.alloc_sbuf_tensor("sb_v", [128, 16 * kvh], BF16)
    sb_aT = nc.alloc_sbuf_tensor("sb_aT", [128, 16 * R], BF16)
    sb_mask = nc.alloc_sbuf_tensor("sb_mask", [128, 512], BF16)
    sb_ident = nc.alloc_sbuf_tensor("sb_ident", [128, 128], BF16)
    sb_m = nc.alloc_sbuf_tensor("sb_m", [128, 4], F32)
    sb_P = nc.alloc_sbuf_tensor("sb_P", [128, 2048], BF16)
    sb_PT = nc.alloc_sbuf_tensor("sb_PT", [128, 16 * 128], BF16)
    sb_w = nc.alloc_sbuf_tensor("sb_w0", [128, DCH * 512], BF16)
    sb_xc = nc.alloc_sbuf_tensor("sb_xc", [128, 512], F32)
    sb_o = nc.alloc_sbuf_tensor("sb_o", [128, 512], F32)
    ps_S = nc.alloc_psum_tensor("ps_S", [128, 2048], F32)
    ps_pt = nc.alloc_psum_tensor("ps_pt", [128, 128], BF16)
    ps_av = nc.alloc_psum_tensor("ps_av", [128, 128], F32)
    ps_o = nc.alloc_psum_tensor("ps_o", [128, 512], F32)

    p.op("sync", lambda e: e.dma_start(sb_ident[:], ident_in),
         writes=["ident"], sem="ld_id")
    for h in range(16):
        p.op("sync", lambda e, h=h: e.dma_start(
            sb_qT[:, h * R:(h + 1) * R], qT_in[h]),
            writes=["qT"], sem="ld_q")

    for hg in range(2):
        heads = range(hg * HG, (hg + 1) * HG)
        for j in range(NT):
            nch = CHUNKS[j]
            kvlen = nch * 512
            kset = sorted(set(h // (16 // nkv) for h in heads))
            for i, hk in enumerate(kset):
                p.op("sync", lambda e, hk=hk, i=i, kvlen=kvlen: e.dma_start(
                    sb_kT[:, i * kvlen:(i + 1) * kvlen],
                    kT_in[hk, :, 0:kvlen]), writes=["kT"], sem="ld_k")
            vbase = hg * kvh if kvd > 512 else hg * 256
            for kt in range(kvlen // 128):
                p.op("gpsimd", lambda e, kt=kt, vbase=vbase: e.dma_start(
                    sb_v[:, kt * kvh:(kt + 1) * kvh],
                    v_in[kt * 128:(kt + 1) * 128, vbase:vbase + kvh]),
                    writes=["v"], sem="ld_v")
            p.op("sync", lambda e, j=j: e.dma_start(sb_mask[:], mask_in[j]),
                 writes=["mask"], sem="ld_m")
            for h in heads:
                hk = h // (16 // nkv)
                ki = kset.index(hk)
                for ci in range(nch):
                    p.op("tensor", lambda e, h=h, ci=ci, ki=ki, j=j,
                         kvlen=kvlen: e.matmul(
                        ps_S[:, ci * 512:(ci + 1) * 512],
                        sb_qT[:, h * R + j * 128: h * R + (j + 1) * 128],
                        sb_kT[:, ki * kvlen + ci * 512:
                              ki * kvlen + (ci + 1) * 512],
                        start=True, stop=True),
                        reads=["qT", "kT"],
                        writes_pre=(["ps_S"] if ci == 0 else ()),
                        writes_post=(["ps_S"] if ci == nch - 1 else ()))
                p.op("vector", lambda e, nch=nch: e.tensor_tensor(
                    ps_S[:, (nch - 1) * 512: nch * 512],
                    ps_S[:, (nch - 1) * 512: nch * 512], sb_mask[:], ALU.add),
                    reads=["mask", "ps_S"], writes=["ps_S"])
                p.op("vector", lambda e, kvlen=kvlen: e.reduce_max(
                    sb_m[:, 0:1], ps_S[:, 0:kvlen], AX.X),
                    reads=["ps_S"], writes=["m0"])
                p.op("vector", lambda e: e.tensor_scalar(
                    sb_m[:, 1:2], sb_m[:, 0:1], -1.0, None, ALU.mult),
                    reads=["m0"], writes=["m1"])
                p.op("scalar", lambda e, kvlen=kvlen: e.activation(
                    sb_P[:, 0:kvlen], ps_S[:, 0:kvlen], AF.Exp,
                    bias=sb_m[:, 1:2], scale=1.0, accum_out=sb_m[:, 2:3]),
                    reads=["ps_S", "m1"], writes=["P", "m2"])
                p.op("vector", lambda e: e.reciprocal(sb_m[:, 3:4],
                                                      sb_m[:, 2:3]),
                     reads=["m2"], writes=["m3"])
                p.op("vector", lambda e, kvlen=kvlen: e.tensor_scalar(
                    sb_P[:, 0:kvlen], sb_P[:, 0:kvlen], sb_m[:, 3:4], None,
                    ALU.mult), reads=["P", "m3"], writes=["P"])
                for kt in range(kvlen // 128):
                    p.op("tensor", lambda e, kt=kt: e.transpose(
                        ps_pt[:], sb_P[:, kt * 128:(kt + 1) * 128],
                        sb_ident[:]),
                        reads=["P", "ident"], writes=["ps_pt"])
                    p.op("vector", lambda e, kt=kt: e.tensor_copy(
                        sb_PT[:, kt * 128:(kt + 1) * 128], ps_pt[:]),
                        reads=["ps_pt"], writes=[("PT", kt), "ps_pt"])
                hl = (h % HG) if kvd > 512 else (hk - kset[0])
                for kt in range(kvlen // 128):
                    p.op("tensor", lambda e, kt=kt, hl=hl, kvlen=kvlen:
                         e.matmul(
                        ps_av[:],
                        sb_v[:, kt * kvh + hl * 128: kt * kvh + (hl + 1) * 128],
                        sb_PT[:, kt * 128:(kt + 1) * 128],
                        start=(kt == 0), stop=(kt == kvlen // 128 - 1)),
                        reads=[("PT", kt), "v"],
                        writes_pre=(["ps_av"] if kt == 0 else ()),
                        writes_post=(["ps_av"] if kt == kvlen // 128 - 1
                                     else ()))
                p.op("vector", lambda e, h=h, j=j: e.tensor_copy(
                    sb_aT[:, h * R + j * 128: h * R + (j + 1) * 128],
                    ps_av[:]),
                    reads=["ps_av"], writes=[("aT", h, j), "ps_av"])
    for oc in range(4):
        p.op("sync", lambda e, oc=oc: e.dma_start(
            sb_w[:], wo_in[:, oc * DCH * 512:(oc + 1) * DCH * 512]),
            writes=["wbuf"], sem="ld_w")
        for j in range(NT):
            for i in range(DCH):
                p.op("tensor", lambda e, i=i, j=j: e.matmul(
                    ps_o[:], sb_aT[:, i * R + j * 128: i * R + (j + 1) * 128],
                    sb_w[:, i * 512:(i + 1) * 512],
                    start=(i == 0), stop=(i == DCH - 1)),
                    reads=["wbuf", ("aT", i, j)],
                    writes_pre=(["ps_o"] if i == 0 else ()),
                    writes_post=(["ps_o"] if i == DCH - 1 else ()))
            p.op("sync", lambda e, j=j, oc=oc: e.dma_start(
                sb_xc[:], x_in[j * 128:(j + 1) * 128,
                               oc * 512:(oc + 1) * 512]),
                writes=["xc"], sem="ld_xc")
            p.op("vector", lambda e: e.tensor_tensor(
                sb_o[:], ps_o[:], sb_xc[:], ALU.add),
                reads=["ps_o", "xc"], writes=["o", "ps_o", "xc"])
            p.op("sync", lambda e, j=j, oc=oc: e.dma_start(
                x_out[j * 128:(j + 1) * 128, oc * 512:(oc + 1) * 512],
                sb_o[:]), reads=["o"], writes=[("xo", j, oc)], sem="st_xo")
    p.fence_all_dma("sync")
    return p.finalize()


def rope_tables(rows, scale):
    invf = 1.0 / 10000.0 ** (np.arange(0, 128, 2, dtype=np.float64) / 128.0)
    fr = rows.astype(np.float64)[None, :] * invf[:, None]
    return (np.cos(fr) * scale).astype(np.float32), \
           (np.sin(fr) * scale).astype(np.float32)


def pack_ocmajor(w):
    k, n = w.shape
    r = w.reshape(k // 128, 128, n // 128, 128)
    return np.ascontiguousarray(
        r.transpose(1, 2, 0, 3).reshape(128, (k * n) // 128))


def pack_vcmajor(w):
    k, n = w.shape
    r = w.reshape(k // 128, 128, n // 512, 512)
    return np.ascontiguousarray(
        r.transpose(1, 2, 0, 3).reshape(128, (k * n) // 128))


def make_masks(c):
    tiles = G_EVEN if c % 2 == 0 else G_ODD
    m = np.zeros((8, 128, 512), np.float32)
    for j, g in enumerate(tiles):
        base = (CHUNKS[j] - 1) * 512
        kcol = base + np.arange(512)[None, :]
        valid = kcol <= (g * 128 + np.arange(128)[:, None])
        m[j] = np.where(valid, 0.0, NEG)
    return m.astype(BF)


def attn_launch(x_by_core, wb, layer):
    kvd = 512 if layer == 0 else 2048
    nkv = kvd // 128
    if layer == 0:
        wqp = pack_ocmajor(np.asarray(wb["l0wq"]))
        wkp = pack_ocmajor(np.asarray(wb["l0wk"]))
        wvp = pack_vcmajor(np.asarray(wb["l0wv"]))
        wop = pack_vcmajor(np.asarray(wb["l0wo"]))
    else:
        wqp = pack_ocmajor(np.asarray(wb["l1wq"]))
        wkp = pack_ocmajor(np.asarray(wb["l1wk"]))
        wvp = pack_vcmajor(np.asarray(wb["l1wv"]))
        wop = pack_vcmajor(np.asarray(wb["l1wo"]))
    s = 1.0 / np.sqrt(128.0)
    nc_a = build_attn_a(kvd)
    in_maps = []
    for c in range(8):
        rows = core_rows(c)
        cq, sq = rope_tables(rows, s)
        ck, sk = rope_tables(rows, 1.0)
        in_maps.append({"x": x_by_core[c], "wq": wqp, "wk": wkp, "wv": wvp,
                        "cosq": cq, "sinq": sq, "cosk": ck, "sink": sk,
                        "ident": IDENT128})
    res = run(nc_a, in_maps)
    qT = [res.results[c]["qT"] for c in range(8)]
    kT = [res.results[c]["kT"] for c in range(8)]
    v = [res.results[c]["v"] for c in range(8)]
    kT_full = []
    v_full = []
    for b in range(B):
        kf = np.zeros((nkv, 128, T), BF)
        vf = np.zeros((T, kvd), BF)
        for half in range(2):
            c = 2 * b + half
            rows = core_rows(c)
            kf[:, :, rows] = kT[c]
            vf[rows, :] = v[c]
        kT_full.append(kf)
        v_full.append(vf)
    nc_b = build_attn_b(kvd)
    in_maps = []
    for c in range(8):
        in_maps.append({"x": x_by_core[c], "qT": qT[c],
                        "kT": kT_full[c // 2], "v": v_full[c // 2],
                        "wo": wop, "mask": make_masks(c),
                        "ident": IDENT128})
    res = run(nc_b, in_maps)
    return [res.results[c]["xo"] for c in range(8)]


def build_l1ffn():
    p = Prog()
    nc = p.nc
    x_in = nc.dram_tensor("x", [R, D], F32, kind="ExternalInput").ap()
    gate_in = nc.dram_tensor("gateP", [128, DCH * 8], BF16,
                             kind="ExternalInput").ap()
    w1_in = nc.dram_tensor("w1P", [9 * 2 * 128, 8192], BF16,
                           kind="ExternalInput").ap()
    w3_in = nc.dram_tensor("w3P", [9 * 2 * 128, 8192], BF16,
                           kind="ExternalInput").ap()
    w2_in = nc.dram_tensor("w2P", [9 * 2 * 128, 8192], BF16,
                           kind="ExternalInput").ap()
    ident_in = nc.dram_tensor("ident", [128, 128], BF16,
                              kind="ExternalInput").ap()
    x_out = nc.dram_tensor("xo", [R, D], F32, kind="ExternalOutput").ap()

    sb_x = nc.alloc_sbuf_tensor("sb_x", [128, NT * D], F32)
    sb_xn = nc.alloc_sbuf_tensor("sb_xn", [128, D], BF16)
    sb_sq = nc.alloc_sbuf_tensor("sb_sq", [128, D], F32)
    sb_st = nc.alloc_sbuf_tensor("sb_st", [128, 8], F32)
    sb_xnT = nc.alloc_sbuf_tensor("sb_xnT", [128, NT * D], BF16)
    sb_ident = nc.alloc_sbuf_tensor("sb_ident", [128, 128], BF16)
    sb_gate = nc.alloc_sbuf_tensor("sb_gate", [128, DCH * 8], BF16)
    sb_sg = nc.alloc_sbuf_tensor("sb_sg", [128, NT * 32], F32)
    sb_top = nc.alloc_sbuf_tensor("sb_top", [128, 8], F32)
    sb_w1 = nc.alloc_sbuf_tensor("sb_w1", [128, 8192], BF16)
    sb_w3 = nc.alloc_sbuf_tensor("sb_w3", [128, 8192], BF16)
    sb_w2 = nc.alloc_sbuf_tensor("sb_w2", [128, 8192], BF16)
    sb_a = nc.alloc_sbuf_tensor("sb_a", [128, 512], F32)
    sb_b = nc.alloc_sbuf_tensor("sb_b", [128, 512], F32)
    sb_h = nc.alloc_sbuf_tensor("sb_h", [128, 512], BF16)
    sb_hT = nc.alloc_sbuf_tensor("sb_hT", [128, 512], BF16)
    ps_tr = nc.alloc_psum_tensor("ps_tr", [128, 128], BF16)
    ps_g = nc.alloc_psum_tensor("ps_g", [128, 8], F32)
    ps_u = nc.alloc_psum_tensor("ps_u", [128, 512], F32)
    ps_v = nc.alloc_psum_tensor("ps_v", [128, 512], F32)
    ps_out = nc.alloc_psum_tensor("ps_out", [128, 2048], F32)

    p.op("sync", lambda e: e.dma_start(sb_ident[:], ident_in),
         writes=["ident"], sem="ld_id")
    p.op("sync", lambda e: e.dma_start(sb_gate[:], gate_in),
         writes=["gate"], sem="ld_g")
    for t in range(NT):
        p.op("sync", lambda e, t=t: e.dma_start(
            sb_x[:, t * D:(t + 1) * D], x_in[t * 128:(t + 1) * 128, :]),
            writes=[("x", t)], sem=("ld_x", t))

    for t in range(NT):
        xt = sb_x[:, t * D:(t + 1) * D]
        emit_ln(p, xt, ("x", t), sb_xn, sb_sq, sb_st)
        for i in range(DCH):
            p.op("tensor", lambda e, i=i, t=t: e.transpose(
                ps_tr[:], sb_xn[:, i * 128:(i + 1) * 128], sb_ident[:]),
                reads=["xn", "ident"], writes=["ps_tr"])
            p.op("vector", lambda e, i=i, t=t: e.tensor_copy(
                sb_xnT[:, t * D + i * 128: t * D + (i + 1) * 128], ps_tr[:]),
                reads=["ps_tr"], writes=[("xnT", t, i), "ps_tr"])
        for dc in range(DCH):
            p.op("tensor", lambda e, t=t, dc=dc: e.matmul(
                ps_g[:], sb_xnT[:, t * D + dc * 128: t * D + (dc + 1) * 128],
                sb_gate[:, dc * 8:(dc + 1) * 8],
                start=(dc == 0), stop=(dc == DCH - 1)),
                reads=[("xnT", t, dc), "gate"],
                writes_pre=(["ps_g"] if dc == 0 else ()),
                writes_post=(["ps_g"] if dc == DCH - 1 else ()))
        
        p.op("scalar", lambda e, t=t: e.activation(
            sb_sg[:, t * 32: t * 32 + 8], ps_g[:], AF.Sigmoid,
            bias=0.0, scale=1.0), reads=["ps_g"],
            writes=[("sgr", t), "ps_g"])
        p.op("vector", lambda e, t=t: e.max(
            out=sb_top[:], in_=sb_sg[:, t * 32: t * 32 + 8]),
            reads=[("sgr", t)], writes=["top"])
        p.op("vector", lambda e, t=t: e.tensor_scalar(
            sb_sg[:, t * 32 + 16: t * 32 + 24],
            sb_sg[:, t * 32: t * 32 + 8], sb_top[:, 1:2], None, ALU.is_ge),
            reads=[("sgr", t), "top"], writes=[("msk", t)])
        p.op("vector", lambda e, t=t: e.tensor_tensor(
            sb_sg[:, t * 32: t * 32 + 8], sb_sg[:, t * 32: t * 32 + 8],
            sb_sg[:, t * 32 + 16: t * 32 + 24], ALU.mult),
            reads=[("msk", t), ("sgr", t)], writes=[("sg", t)])
        p.op("vector", lambda e, t=t: e.memset(
            sb_sg[:, t * 32 + 8: t * 32 + 9], 1.0), writes=[("sg1", t)])

    for eidx in range(9):
        for half in range(2):
            row0 = (eidx * 2 + half) * 128
            p.op("sync", lambda e, row0=row0: e.dma_start(
                sb_w1[:], w1_in[row0:row0 + 128, :]), writes=["w1"],
                sem="ld_w1")
            p.op("sync", lambda e, row0=row0: e.dma_start(
                sb_w3[:], w3_in[row0:row0 + 128, :]), writes=["w3"],
                sem="ld_w3")
            p.op("gpsimd", lambda e, row0=row0: e.dma_start(
                sb_w2[:], w2_in[row0:row0 + 128, :]), writes=["w2"],
                sem="ld_w2")
            for t in range(NT):
                xt = sb_x[:, t * D:(t + 1) * D]
                scol = sb_sg[:, t * 32 + eidx: t * 32 + eidx + 1]
                sreg = ("sg", t) if eidx < 8 else ("sg1", t)
                for dc in range(DCH):
                    p.op("tensor", lambda e, t=t, dc=dc: e.matmul(
                        ps_u[:],
                        sb_xnT[:, t * D + dc * 128: t * D + (dc + 1) * 128],
                        sb_w1[:, dc * 512:(dc + 1) * 512],
                        start=(dc == 0), stop=(dc == DCH - 1)),
                        reads=[("xnT", t, dc), "w1"],
                        writes_pre=(["ps_u"] if dc == 0 else ()),
                        writes_post=(["ps_u"] if dc == DCH - 1 else ()))
                for dc in range(DCH):
                    p.op("tensor", lambda e, t=t, dc=dc: e.matmul(
                        ps_v[:],
                        sb_xnT[:, t * D + dc * 128: t * D + (dc + 1) * 128],
                        sb_w3[:, dc * 512:(dc + 1) * 512],
                        start=(dc == 0), stop=(dc == DCH - 1)),
                        reads=[("xnT", t, dc), "w3"],
                        writes_pre=(["ps_v"] if dc == 0 else ()),
                        writes_post=(["ps_v"] if dc == DCH - 1 else ()))
                p.op("vector", lambda e, scol=scol: e.tensor_scalar(
                    sb_a[:], ps_u[:], scol, None, ALU.mult),
                    reads=["ps_u", sreg], writes=["a", "ps_u"])
                p.op("scalar", lambda e: e.activation(
                    sb_a[:], sb_a[:], AF.Silu, bias=0.0, scale=1.0),
                    reads=["a"], writes=["a"])
                p.op("vector", lambda e, scol=scol: e.tensor_scalar(
                    sb_b[:], ps_v[:], scol, None, ALU.mult),
                    reads=["ps_v", sreg], writes=["b", "ps_v"])
                p.op("vector", lambda e: e.tensor_tensor(
                    sb_h[:], sb_a[:], sb_b[:], ALU.mult),
                    reads=["a", "b"], writes=["h"])
                for hc in range(4):
                    p.op("tensor", lambda e, hc=hc: e.transpose(
                        ps_tr[:], sb_h[:, hc * 128:(hc + 1) * 128],
                        sb_ident[:]),
                        reads=["h", "ident"], writes=["ps_tr"])
                    p.op("vector", lambda e, hc=hc: e.tensor_copy(
                        sb_hT[:, hc * 128:(hc + 1) * 128], ps_tr[:]),
                        reads=["ps_tr"], writes=[("hT", hc), "ps_tr"])
                for oc in range(4):
                    for hc in range(4):
                        p.op("tensor", lambda e, oc=oc, hc=hc: e.matmul(
                            ps_out[:, oc * 512:(oc + 1) * 512],
                            sb_hT[:, hc * 128:(hc + 1) * 128],
                            sb_w2[:, oc * 2048 + hc * 512:
                                  oc * 2048 + (hc + 1) * 512],
                            start=(hc == 0), stop=(hc == 3)),
                            reads=[("hT", hc), "w2"],
                            writes_pre=(["ps_out"] if (oc == 0 and hc == 0)
                                        else ()),
                            writes_post=(["ps_out"] if (oc == 3 and hc == 3)
                                         else ()))
                p.op("vector", lambda e, xt=xt, t=t: e.scalar_tensor_tensor(
                    xt, ps_out[:], 0.5, xt, ALU.mult, ALU.add),
                    reads=["ps_out"], writes=[("x", t), "ps_out"])
    for t in range(NT):
        p.op("sync", lambda e, t=t: e.dma_start(
            x_out[t * 128:(t + 1) * 128, :], sb_x[:, t * D:(t + 1) * D]),
            reads=[("x", t)], writes=[("xo", t)], sem="st_xo")
    p.fence_all_dma("sync")
    return p.finalize()


def l1ffn_weights(wb):
    gateP = pack_kmajor(np.asarray(wb["gateT"]))
    w1T = np.asarray(wb["w1T"]).reshape(8, D, 1024)
    w3T = np.asarray(wb["w3T"]).reshape(8, D, 1024)
    w2T = np.asarray(wb["w2T"]).reshape(8, 1024, D)
    sw1T = np.asarray(wb["sw1T"])
    sw3T = np.asarray(wb["sw3T"])
    sw2T = np.asarray(wb["sw2T"])
    W1 = np.zeros((9 * 2 * 128, 8192), BF)
    W3 = np.zeros((9 * 2 * 128, 8192), BF)
    W2 = np.zeros((9 * 2 * 128, 8192), BF)
    for eidx in range(9):
        a1 = w1T[eidx] if eidx < 8 else sw1T
        a3 = w3T[eidx] if eidx < 8 else sw3T
        a2 = w2T[eidx] if eidx < 8 else sw2T
        p1 = pack_vcmajor(a1)
        p3 = pack_vcmajor(a3)
        for half in range(2):
            r0 = (eidx * 2 + half) * 128
            W1[r0:r0 + 128] = p1[:, half * 8192:(half + 1) * 8192]
            W3[r0:r0 + 128] = p3[:, half * 8192:(half + 1) * 8192]
            W2[r0:r0 + 128] = pack_vcmajor(
                a2[half * 512:(half + 1) * 512, :])
    return {"gateP": gateP, "w1P": W1, "w3P": W3, "w2P": W2}


def l1ffn_launch(x_by_core, l1w):
    nc = build_l1ffn()
    in_maps = []
    for c in range(8):
        in_maps.append({"x": x_by_core[c], "ident": IDENT128, **l1w})
    res = run(nc, in_maps)
    return [res.results[c]["xo"] for c in range(8)]
